# version 01
# baseline (speedup 1.0000x reference)
import contextlib
import numpy as np
import ml_dtypes
import concourse.bass as bass
import concourse.mybir as mybir
from concourse.bass_utils import run_bass_kernel_spmd

F32 = mybir.dt.float32
BF16 = mybir.dt.bfloat16
AF = mybir.ActivationFunctionType
ALU = mybir.AluOpType
BF = ml_dtypes.bfloat16

D = 1024
B = 4
S = 4096
NTOK = B * S
NCORES = 8
TPC = NTOK // NCORES
TT = 512
NKC = D // 128
NTT = TPC // TT



class Em:

    STREAMS = ("tensor", "vector", "scalar", "sync", "gpsimd", "scalar_dma",
               "coll")

    def __init__(self, nc):
        self.nc = nc
        self.q = {"tensor": [], "vector": [], "scalar": [], "sync": [],
                  "gpsimd": []}
        self.sems = {}
        self.cnt = {s: 0 for s in self.STREAMS}
        self.wm = {}

    def init_sems(self, ctx):
        for s in self.STREAMS:
            self.sems[s] = ctx.enter_context(self.nc.semaphore("sem_" + s))

    @staticmethod
    def queue_of(stream):
        if stream == "coll":
            return "gpsimd"
        if stream == "scalar_dma":
            return "scalar"
        return stream

    def emit(self, stream, fn, deps=(), inc=0):
        qname = self.queue_of(stream)
        for dep in deps:
            if dep is None:
                continue
            dstream, dval = dep
            if self.queue_of(dstream) == qname:
                continue
            key = (qname, dstream)
            if self.wm.get(key, 0) >= dval:
                continue
            self.wm[key] = dval
            sem = self.sems[dstream]
            self.q[qname].append(lambda e, sem=sem, v=dval: e.wait_ge(sem, v))
        tok = None
        if inc:
            self.cnt[stream] += inc
            tok = (stream, self.cnt[stream])
            sem = self.sems[stream]
            self.q[qname].append(
                lambda e, fn=fn, sem=sem, inc=inc: fn(e).then_inc(sem, inc))
        else:
            self.q[qname].append(lambda e, fn=fn: fn(e))
        return tok

    def dma(self, qname, out, in_, deps=(), inc=True, **kw):
        stream = "scalar_dma" if qname == "scalar" else qname
        return self.emit(stream,
                         lambda e: e.dma_start(out=out, in_=in_, **kw),
                         deps, inc=16 if inc else 0)

    def mm(self, out, lhsT, rhs, start, stop, deps=(), inc=False, **kw):
        return self.emit("tensor",
                         lambda e: e.matmul(out, lhsT, rhs, start=start,
                                            stop=stop, **kw),
                         deps, inc=1 if inc else 0)

    def act(self, out, in_, func, deps=(), inc=False, **kw):
        return self.emit("scalar",
                         lambda e: e.activation(out, in_, func, **kw),
                         deps, inc=1 if inc else 0)

    def vec(self, meth, *args, deps=(), inc=False, **kw):
        return self.emit("vector",
                         lambda e: getattr(e, meth)(*args, **kw),
                         deps, inc=1 if inc else 0)

    def scal(self, meth, *args, deps=(), inc=False, **kw):
        return self.emit("scalar",
                         lambda e: getattr(e, meth)(*args, **kw),
                         deps, inc=1 if inc else 0)

    def bar(self, stream):
        return (stream, self.cnt[stream])

    def finalize(self, block):
        em = self

        @block.tensor
        def _(e):
            for fn in em.q["tensor"]:
                fn(e)

        @block.vector
        def _(e):
            for fn in em.q["vector"]:
                fn(e)

        @block.scalar
        def _(e):
            for fn in em.q["scalar"]:
                fn(e)

        @block.sync
        def _(e):
            for fn in em.q["sync"]:
                fn(e)

        @block.gpsimd
        def _(e):
            for fn in em.q["gpsimd"]:
                fn(e)


class WBuf:

    def __init__(self, em, bufs):
        self.em = em
        self.bufs = bufs
        self.n = len(bufs)
        self.ld = [None] * self.n
        self.use = [None] * self.n
        self.i = 0

    def load(self, hbm_ap, dst_fn=None):
        i = self.i % self.n
        self.i += 1
        dst = dst_fn(self.bufs[i]) if dst_fn else self.bufs[i][:]
        self.ld[i] = self.em.dma("gpsimd", dst, hbm_ap,
                                 deps=[self.use[i]] if self.use[i] else [])
        return i

    def buf(self, i):
        return self.bufs[i]

    def tok(self, i):
        return self.ld[i]

    def used(self, i, tok):
        self.use[i] = tok


class PsPool:

    def __init__(self, tensors):
        self.t = tensors
        self.use = [None] * len(tensors)
        self.i = 0

    def next(self):
        i = self.i % len(self.t)
        self.i += 1
        return i, self.t[i], self.use[i]

    def done(self, i, tok):
        self.use[i] = tok



def to_fm(x2d):
    Dd, T = x2d.shape
    C = Dd // 128
    return np.ascontiguousarray(x2d.reshape(C, 128, T).transpose(1, 0, 2))


def from_fm(img):
    P, C, T = img.shape
    return np.ascontiguousarray(img.transpose(1, 0, 2).reshape(C * P, T))


def tile_w(wT, blocks):
    K, N = wT.shape
    KC = K // 128
    assert K % 128 == 0
    outs = []
    for (n0, n1) in blocks:
        w = n1 - n0
        blk = wT[:, n0:n1].reshape(KC, 128, w).transpose(1, 0, 2).reshape(
            128, KC * w)
        outs.append(blk)
    return np.ascontiguousarray(np.concatenate(outs, axis=1))


def col_img(v):
    N = v.shape[0]
    C = N // 128
    return np.ascontiguousarray(v.reshape(C, 128).T.astype(np.float32))


def l0_rope_tables(positions):
    T = len(positions)
    hd = 64
    i = np.arange(hd // 2, dtype=np.float64)
    inv = 1.0 / (10000.0 ** (2 * i / hd))
    ang = positions[:, None].astype(np.float64) * inv[None, :]
    cos = np.cos(ang)
    sin = np.sin(ang)
    Cc = np.empty((hd, T))
    Ss = np.empty((hd, T))
    Cc[0::2] = cos.T
    Cc[1::2] = cos.T
    Ss[0::2] = -sin.T
    Ss[1::2] = sin.T
    return (np.tile(Cc, (2, 1)).astype(np.float32),
            np.tile(Ss, (2, 1)).astype(np.float32))


def pair_swap_matrix():
    P = np.zeros((128, 128), dtype=np.float32)
    for i in range(0, 128, 2):
        P[i, i + 1] = 1.0
        P[i + 1, i] = 1.0
    return P.astype(BF)


def untile_out(img):
    P, ntt, nkc, ttt = img.shape
    return np.ascontiguousarray(
        img.transpose(2, 0, 1, 3).reshape(nkc * P, ntt * ttt))



def emit_ln(em, ps_mu, ps_var, ps_brstd, ps_bmur, ones128, ones1r, rowbuf,
            sqbuf, eps_col, x_chunks_w, T, xn_out_fn, x_dep, state,
            xn_pad=None, eps=1e-5, gain_bias=None):
    TW = max(T, 128)
    musq_r = rowbuf[:, 0:TW]
    var_r = rowbuf[:, TW:2 * TW]
    ln_r = rowbuf[:, 2 * TW:3 * TW]
    rstd_r = rowbuf[:, 3 * TW:4 * TW]
    mur_r = rowbuf[:, 4 * TW:5 * TW]
    var_tok = None
    for kc in range(NKC):
        sqt = em.act(sqbuf[kc % 2][:, :TW], x_chunks_w[kc][:, :TW], AF.Square,
                     deps=[x_dep, state.get(("sq", kc % 2))], inc=True)
        em.mm(ps_mu[:, :TW], ones128[:], x_chunks_w[kc][:, :TW], kc == 0,
              kc == NKC - 1,
              deps=[x_dep, state.get("mu_free")] if kc == 0 else [])
        mtok = em.mm(ps_var[:, :TW], ones128[:], sqbuf[kc % 2][:, :TW], kc == 0,
                     kc == NKC - 1,
                     deps=([sqt, state.get("var_free")] if kc == 0 else [sqt]),
                     inc=True)
        state[("sq", kc % 2)] = mtok
        if kc == NKC - 1:
            var_tok = mtok
    mst = em.act(musq_r, ps_mu[:, :TW], AF.Square, scale=1.0 / D,
                 deps=[var_tok], inc=True)
    vt = em.vec("scalar_tensor_tensor", var_r, ps_var[:, :TW], 1.0 / D,
                musq_r, ALU.mult, ALU.subtract, deps=[mst], inc=True)
    em.act(ln_r, var_r, AF.Ln, bias=eps_col[:], deps=[vt])
    rt = em.act(rstd_r, ln_r, AF.Exp, scale=-0.5, inc=True)
    state["mu_free"] = rt
    state["var_free"] = vt
    mrt = em.vec("scalar_tensor_tensor", mur_r, ps_mu[:, :TW], 1.0 / D,
                 rstd_r, ALU.mult, ALU.mult, deps=[rt], inc=True)
    bdep = state.get("b_free")
    bt1 = em.mm(ps_brstd[:, :TW], ones1r[:], rstd_r, True, True,
                deps=[rt, bdep], inc=True)
    bt2 = em.mm(ps_bmur[:, :TW], ones1r[:], mur_r, True, True,
                deps=[mrt, bdep], inc=True)
    state["mu_free"] = mrt
    xn_free = state.get("xn_free") or []
    tmp = state["tmp128"]
    xn_tok = None
    direct = T >= 128
    for kc in range(NKC):
        dst = xn_out_fn(kc) if direct else xn_pad[:, kc, :TW]
        em.vec("tensor_mul", tmp[:, :TW], x_chunks_w[kc][:, :TW],
               ps_brstd[:, :TW], deps=[bt1, bt2] + list(xn_free))
        if gain_bias is None:
            xn_tok = em.vec("tensor_sub", dst, tmp[:, :TW], ps_bmur[:, :TW],
                            inc=(kc == NKC - 1 and direct))
        else:
            g_col, b_col = gain_bias
            em.vec("tensor_sub", tmp[:, :TW], tmp[:, :TW], ps_bmur[:, :TW])
            xn_tok = em.vec("tensor_scalar", dst, tmp[:, :TW],
                            g_col[:, kc:kc + 1], b_col[:, kc:kc + 1],
                            ALU.mult, ALU.add,
                            inc=(kc == NKC - 1 and direct))
    if not direct:
        for kc in range(NKC):
            xn_tok = em.vec("tensor_copy", xn_out_fn(kc), xn_pad[:, kc, :T],
                            inc=(kc == NKC - 1))
    state["b_free"] = xn_tok
    return xn_tok



def build_p1():
    nc = bass.Bass("TRN2", target_bir_lowering=False)
    x_h = nc.dram_tensor("x", [128, NTT, NKC, TT], F32, kind="ExternalInput").ap()
    wqkv_h = nc.dram_tensor("wqkv", [128, 3 * NKC * NKC * 128], F32,
                            kind="ExternalInput").ap()
    w12_h = nc.dram_tensor("w12", [128, 2 * 32 * NKC * 128], F32,
                           kind="ExternalInput").ap()
    h1_h = nc.dram_tensor("h1", [128, 32 * 32 * 128], F32,
                          kind="ExternalInput").ap()
    h2_h = nc.dram_tensor("h2", [128, 32 * 32 * 128], F32,
                          kind="ExternalInput").ap()
    w3_h = nc.dram_tensor("w3", [128, NKC * 32 * 128], F32,
                          kind="ExternalInput").ap()
    s1_h = nc.dram_tensor("s1", [128, 32], F32, kind="ExternalInput").ap()
    s2_h = nc.dram_tensor("s2", [128, 32], F32, kind="ExternalInput").ap()
    sh1_h = nc.dram_tensor("sh1", [128, 32], F32, kind="ExternalInput").ap()
    sh2_h = nc.dram_tensor("sh2", [128, 32], F32, kind="ExternalInput").ap()
    s3_h = nc.dram_tensor("s3", [128, NKC], F32, kind="ExternalInput").ap()
    ropeC_h = nc.dram_tensor("ropeC", [128, TPC], F32, kind="ExternalInput").ap()
    ropeS_h = nc.dram_tensor("ropeS", [128, TPC], F32, kind="ExternalInput").ap()
    pswap_h = nc.dram_tensor("pswap", [128, 128], BF16, kind="ExternalInput").ap()

    q_o = nc.dram_tensor("q", [128, NTT, NKC, TT], BF16, kind="ExternalOutput").ap()
    k_o = nc.dram_tensor("k", [128, NTT, NKC, TT], BF16, kind="ExternalOutput").ap()
    v_o = nc.dram_tensor("v", [128, NTT, NKC, TT], BF16, kind="ExternalOutput").ap()
    r_o = nc.dram_tensor("r", [128, NTT, NKC, TT], F32, kind="ExternalOutput").ap()
    xn_o = nc.dram_tensor("xn", [128, NTT, NKC, TT], BF16,
                          kind="ExternalOutput").ap()

    with contextlib.ExitStack() as ctx:
        em = Em(nc)
        em.init_sems(ctx)
        sb = lambda name, shape, dt: ctx.enter_context(
            nc.sbuf_tensor(name, shape, dt))
        psa = lambda name, shape, dt=F32: ctx.enter_context(
            nc.psum_tensor(name, shape, dt))

        xt = [sb(f"xt{i}", [128, NKC, TT], F32) for i in range(2)]
        xn = sb("xn_sb", [128, NKC, TT], BF16)
        sqbuf = [sb(f"sq{i}", [128, TT], F32) for i in range(2)]
        ones128 = sb("ones128", [128, 1], F32)
        ones1r = sb("ones1r", [1, 128], F32)
        eps_col = sb("eps_col", [1, 1], F32)
        rowbuf = sb("rowbuf", [1, 6 * TT], F32)
        ropeC = sb("ropeC_sb", [128, TT], F32)
        ropeS = sb("ropeS_sb", [128, TT], F32)
        pswap = sb("pswap_sb", [128, 128], BF16)
        scol = {n: sb(f"scol_{n}", [128, 32], F32) for n in
                ("s1", "s2", "sh1", "sh2")}
        s3col = sb("s3col", [128, NKC], F32)
        wq_b = WBuf(em, [sb(f"wq{i}", [128, NKC, 128], BF16) for i in range(2)])
        w12_b = WBuf(em, [sb(f"w12{i}", [128, 2 * NKC, 128], BF16)
                          for i in range(2)])
        wh_b = WBuf(em, [sb(f"wh{i}", [128, 32, 128], BF16) for i in range(3)])
        hbuf = sb("hbuf", [128, 32, TT], BF16)
        gbuf = sb("gbuf", [128, 32, TT], BF16)
        q_sb = sb("qsb", [128, NKC, TT], BF16)
        k_sb = sb("ksb", [128, NKC, TT], BF16)
        v_sb = sb("vsb", [128, NKC, TT], BF16)
        r_sb = sb("rsb", [128, NKC, TT], F32)
        tmp1 = sb("tmp1", [128, TT], F32)
        tmp2 = sb("tmp2", [128, TT], F32)
        silt = sb("silt", [128, TT], F32)

        ps_mu = psa("ps_mu", [1, TT])
        ps_var = psa("ps_var", [1, TT])
        ps_brstd = psa("ps_brstd", [128, TT])
        ps_bmur = psa("ps_bmur", [128, TT])
        pool = PsPool([psa(f"ps_main{i}", [128, TT]) for i in range(3)])
        ps_swap = psa("ps_swap", [128, TT])

        def wqkv_slice(gi):
            off = gi * (NKC * 128)
            return wqkv_h[:, off:off + NKC * 128].rearrange(
                "p (c n) -> p c n", c=NKC)

        def w12_slice(oc):
            off = oc * (2 * NKC * 128)
            return w12_h[:, off:off + 2 * NKC * 128].rearrange(
                "p (c n) -> p c n", c=2 * NKC)

        def h_slice(h_h, oc):
            off = oc * (32 * 128)
            return h_h[:, off:off + 32 * 128].rearrange(
                "p (c n) -> p c n", c=32)

        with nc.Block() as block:
            em.dma("sync", pswap[:], pswap_h[:])
            for n, h in (("s1", s1_h), ("s2", s2_h), ("sh1", sh1_h),
                         ("sh2", sh2_h)):
                em.dma("sync", scol[n][:], h)
            ld0 = em.dma("sync", s3col[:], s3_h[:])
            em.emit("vector", lambda e: e.memset(ones128[:], 1.0), inc=1)
            em.emit("vector", lambda e: e.memset(ones1r[:], 1.0), inc=1)
            em.emit("vector", lambda e: e.memset(eps_col[:], 1e-5), inc=1)
            consts_tok = em.bar("vector")
            const_ld = ld0
            em.emit("tensor", lambda e: e.nop(), deps=[const_ld, consts_tok])
            em.emit("scalar", lambda e: e.nop(), deps=[const_ld, consts_tok])
            em.emit("vector", lambda e: e.nop(), deps=[const_ld])

            state = {"tmp128": tmp1, "mu_free": consts_tok,
                     "var_free": consts_tok, "b_free": consts_tok}
            ps_swap_use = None
            out_done = None
            xn_readers = None
            x_use = [None, None]
            rope_use = None

            wq_b.load(wqkv_slice(0))

            for tt_i in range(NTT):
                xb = tt_i % 2
                x_tok = em.dma("sync", xt[xb][:], x_h[:, tt_i],
                               deps=[x_use[xb]])
                rtab_tok = em.dma("sync", ropeC[:],
                                  ropeC_h[:, tt_i * TT:(tt_i + 1) * TT],
                                  deps=[rope_use])
                rtab_tok2 = em.dma("sync", ropeS[:],
                                   ropeS_h[:, tt_i * TT:(tt_i + 1) * TT],
                                   deps=[rope_use])

                x_chunks = [xt[xb][:, kc, :] for kc in range(NKC)]
                state["xn_free"] = ([xn_readers, out_done]
                                    if xn_readers else None)
                xn_tok = emit_ln(em, ps_mu, ps_var, ps_brstd, ps_bmur, ones128,
                                 ones1r, rowbuf, sqbuf, eps_col, x_chunks, TT,
                                 lambda kc, xn=xn: xn[:, kc, :],
                                 x_tok, state)
                first_deps = []

                qkv_done = None
                for gi in range(3 * NKC):
                    proj, oc = divmod(gi, NKC)
                    bi = (tt_i * 3 * NKC + gi) % 2
                    cur_ld = wq_b.tok(bi)
                    pi, pm, pdep = pool.next()
                    last = None
                    for kc in range(NKC):
                        last = em.mm(pm[:], wq_b.buf(bi)[:, kc, :],
                                     xn[:, kc, :], kc == 0, kc == NKC - 1,
                                     deps=([cur_ld, xn_tok, pdep] + first_deps
                                           if kc == 0 else []),
                                     inc=(kc == NKC - 1))
                    wq_b.used(bi, last)
                    ng = gi + 1
                    if ng < 3 * NKC:
                        wq_b.load(wqkv_slice(ng))
                    elif tt_i < NTT - 1:
                        wq_b.load(wqkv_slice(0))
                    dst = (q_sb, k_sb, v_sb)[proj]
                    if proj == 2:
                        ctok = em.act(dst[:, oc, :], pm[:], AF.Identity,
                                      deps=[last, out_done], inc=True)
                        pool.done(pi, ctok)
                    else:
                        sc = 0.125 if proj == 0 else 1.0
                        etok = em.act(dst[:, oc, :], pm[:], AF.Identity,
                                      scale=sc, deps=[last, out_done],
                                      inc=True)
                        stok = em.mm(ps_swap[:], pswap[:], dst[:, oc, :],
                                     True, True,
                                     deps=[etok, ps_swap_use], inc=True)
                        em.vec("tensor_mul", tmp2[:], dst[:, oc, :],
                               ropeC[:], deps=[etok, rtab_tok, rtab_tok2])
                        sw_rd = em.vec("tensor_mul", tmp1[:], ps_swap[:],
                                       ropeS[:], deps=[stok], inc=True)
                        ps_swap_use = sw_rd
                        ctok = em.vec("tensor_add", dst[:, oc, :], tmp2[:],
                                      tmp1[:], inc=True)
                        pool.done(pi, ctok)
                    qkv_done = ctok
                q_dma = em.dma("sync", q_o[:, tt_i], q_sb[:],
                               deps=[em.bar("vector")])
                k_dma = em.dma("sync", k_o[:, tt_i], k_sb[:],
                               deps=[em.bar("vector")])
                v_dma = em.dma("sync", v_o[:, tt_i], v_sb[:],
                               deps=[em.bar("scalar")])
                xn_dma = em.dma("sync", xn_o[:, tt_i], xn[:], deps=[xn_tok])

                silt_use = None
                for oc in range(32):
                    bi = w12_b.load(w12_slice(oc))
                    pi1, pm1, pd1 = pool.next()
                    pi2, pm2, pd2 = pool.next()
                    l1 = None
                    l2 = None
                    for kc in range(NKC):
                        l1 = em.mm(pm1[:], w12_b.buf(bi)[:, kc, :],
                                   xn[:, kc, :], kc == 0, kc == NKC - 1,
                                   deps=([w12_b.tok(bi), xn_tok, pd1]
                                         if kc == 0 else []),
                                   inc=(kc == NKC - 1))
                    for kc in range(NKC):
                        l2 = em.mm(pm2[:], w12_b.buf(bi)[:, NKC + kc, :],
                                   xn[:, kc, :], kc == 0, kc == NKC - 1,
                                   deps=[pd2] if kc == 0 else [],
                                   inc=(kc == NKC - 1))
                    w12_b.used(bi, l2)
                    st = em.act(silt[:], pm1[:], AF.Silu,
                                deps=[l1, silt_use],
                                scale=scol["s1"][:, oc:oc + 1], inc=True)
                    htok = em.vec("scalar_tensor_tensor", hbuf[:, oc, :],
                                  pm2[:], scol["s2"][:, oc:oc + 1], silt[:],
                                  ALU.mult, ALU.mult, deps=[l2, st], inc=True)
                    silt_use = htok
                    pool.done(pi1, htok)
                    pool.done(pi2, htok)
                h_tok = em.bar("vector")

                for oc in range(32):
                    bi = wh_b.load(h_slice(h1_h, oc))
                    pi, pm, pdep = pool.next()
                    last = None
                    for kc in range(32):
                        last = em.mm(pm[:], wh_b.buf(bi)[:, kc, :],
                                     hbuf[:, kc, :], kc == 0, kc == 31,
                                     deps=([wh_b.tok(bi), h_tok, pdep]
                                           if kc == 0 else []),
                                     inc=(kc == 31))
                    wh_b.used(bi, last)
                    gt = em.act(gbuf[:, oc, :], pm[:], AF.Gelu, deps=[last],
                                scale=scol["sh1"][:, oc:oc + 1], inc=True)
                    pool.done(pi, gt)
                g1_tok = em.bar("scalar")

                for oc in range(32):
                    bi = wh_b.load(h_slice(h2_h, oc))
                    pi, pm, pdep = pool.next()
                    last = None
                    for kc in range(32):
                        last = em.mm(pm[:], wh_b.buf(bi)[:, kc, :],
                                     gbuf[:, kc, :], kc == 0, kc == 31,
                                     deps=([wh_b.tok(bi), g1_tok, pdep]
                                           if kc == 0 else []),
                                     inc=(kc == 31))
                    wh_b.used(bi, last)
                    gt = em.act(hbuf[:, oc, :], pm[:], AF.Gelu, deps=[last],
                                scale=scol["sh2"][:, oc:oc + 1], inc=True)
                    pool.done(pi, gt)
                g2_tok = em.bar("scalar")

                r_done = None
                for oc in range(NKC):
                    bi = wh_b.load(h_slice(w3_h, oc))
                    pi, pm, pdep = pool.next()
                    last = None
                    for kc in range(32):
                        last = em.mm(pm[:], wh_b.buf(bi)[:, kc, :],
                                     hbuf[:, kc, :], kc == 0, kc == 31,
                                     deps=([wh_b.tok(bi), g2_tok, pdep]
                                           if kc == 0 else []),
                                     inc=(kc == 31))
                    wh_b.used(bi, last)
                    rt = em.vec("scalar_tensor_tensor", r_sb[:, oc, :], pm[:],
                                s3col[:, oc:oc + 1], xt[xb][:, oc, :],
                                ALU.mult, ALU.add, deps=[last, out_done],
                                inc=True)
                    pool.done(pi, rt)
                    r_done = rt
                r_dma = em.dma("sync", r_o[:, tt_i], r_sb[:],
                               deps=[em.bar("vector")])
                out_done = r_dma
                xn_readers = em.bar("tensor")
                x_use[xb] = r_dma
                rope_use = em.bar("vector")

            em.emit("gpsimd", lambda e: e.nop(), deps=[em.bar("sync")])
            em.finalize(block)
    return nc


def host_prep_p1(inputs):
    x = np.asarray(inputs["L0_attn_x"], dtype=np.float32)
    xT = x.reshape(NTOK, D).T
    blocks128 = [(i * 128, (i + 1) * 128) for i in range(NKC)]
    wq = np.ascontiguousarray(np.asarray(inputs["L0_attn_Wq"]).T)
    wk = np.ascontiguousarray(np.asarray(inputs["L0_attn_Wk"]).T)
    wv = np.ascontiguousarray(np.asarray(inputs["L0_attn_Wv"]).T)
    wqkv_img = np.concatenate(
        [tile_w(w, blocks128) for w in (wq, wk, wv)], axis=1)
    w1T = np.ascontiguousarray(np.asarray(inputs["L0_ffn_w1_w"]).T)
    w2T = np.ascontiguousarray(np.asarray(inputs["L0_ffn_w2_w"]).T)
    w12_parts = []
    for oc in range(32):
        b = [(oc * 128, (oc + 1) * 128)]
        w12_parts.append(tile_w(w1T, b))
        w12_parts.append(tile_w(w2T, b))
    w12_img = np.ascontiguousarray(np.concatenate(w12_parts, axis=1))
    blocks32 = [(i * 128, (i + 1) * 128) for i in range(32)]
    h1_img = tile_w(np.ascontiguousarray(np.asarray(inputs["L0_ffn_h1_w"]).T),
                    blocks32)
    h2_img = tile_w(np.ascontiguousarray(np.asarray(inputs["L0_ffn_h2_w"]).T),
                    blocks32)
    w3_img = tile_w(np.ascontiguousarray(np.asarray(inputs["L0_ffn_w3_w"]).T),
                    blocks128)
    s1 = col_img(np.asarray(inputs["L0_ffn_w1_s"]))
    s2 = col_img(np.asarray(inputs["L0_ffn_w2_s"]))
    sh1 = col_img(np.asarray(inputs["L0_ffn_h1_s"]))
    sh2 = col_img(np.asarray(inputs["L0_ffn_h2_s"]))
    s3 = col_img(np.asarray(inputs["L0_ffn_w3_s"]))
    pswap = pair_swap_matrix()

    in_maps = []
    for c in range(NCORES):
        sl = slice(c * TPC, (c + 1) * TPC)
        positions = (np.arange(TPC) + (c % 2) * TPC).astype(np.float64)
        Cc, Ss = l0_rope_tables(positions)
        xi = to_fm(xT[:, sl])
        xi = np.ascontiguousarray(
            xi.reshape(128, NKC, NTT, TT).transpose(0, 2, 1, 3))
        in_maps.append({
            "x": xi,
            "wqkv": wqkv_img, "w12": w12_img, "h1": h1_img, "h2": h2_img,
            "w3": w3_img, "s1": s1, "s2": s2, "sh1": sh1, "sh2": sh2, "s3": s3,
            "ropeC": Cc, "ropeS": Ss, "pswap": pswap,
        })
    return in_maps





NPAIR = 8
NQT = S // TT


def build_p2():
    nc = bass.Bass("TRN2", target_bir_lowering=False)
    q_h = nc.dram_tensor("q", [64, NPAIR, S], BF16, kind="ExternalInput").ap()
    k_h = nc.dram_tensor("k", [64, NPAIR, S], BF16, kind="ExternalInput").ap()
    v_h = nc.dram_tensor("v", [128, NPAIR, 32, 65], BF16,
                         kind="ExternalInput").ap()
    m_h = nc.dram_tensor("masks", [128, 4, TT], F32, kind="ExternalInput").ap()
    ctx_o = nc.dram_tensor("ctx", [64, NPAIR, S], BF16,
                           kind="ExternalOutput").ap()

    with contextlib.ExitStack() as ctx:
        em = Em(nc)
        em.init_sems(ctx)
        sb = lambda name, shape, dt: ctx.enter_context(
            nc.sbuf_tensor(name, shape, dt))
        psa = lambda name, shape, dt=F32: ctx.enter_context(
            nc.psum_tensor(name, shape, dt))

        q_sb = [sb(f"q{i}", [64, S], BF16) for i in range(2)]
        k_sb = [sb(f"k{i}", [64, S], BF16) for i in range(2)]
        v_sb = [sb(f"v{i}", [128, 32, 65], BF16) for i in range(2)]
        masks = sb("m_sb", [128, 4, TT], F32)
        expS = [sb(f"expS{i}", [128, TT], BF16) for i in range(2)]
        dn = sb("dn", [65, TT], F32)
        dones = sb("dones", [65, 64], F32)
        bc_sb = sb("bc_sb", [64, TT], F32)
        ctx_sb = sb("ctx_sb", [64, TT], BF16)

        ps_s = [psa(f"ps_s{i}", [128, TT]) for i in range(2)]
        ps_ctx = psa("ps_ctx", [65, TT])
        ps_bc = psa("ps_bc", [64, TT])

        with nc.Block() as block:
            m_ld = em.dma("sync", masks[:], m_h[:])
            em.emit("vector", lambda e: e.memset(dones[:], 1.0), inc=1)
            consts = em.bar("vector")
            em.emit("tensor", lambda e: e.nop(), deps=[m_ld, consts])
            em.emit("vector", lambda e: e.nop(), deps=[m_ld])

            in_use = [None, None]
            in_ld = [None, None]

            def load_pair(h):
                i = h % 2
                dep = [in_use[i]] if in_use[i] else []
                em.dma("sync", q_sb[i][:], q_h[:, h, :], deps=dep)
                em.dma("sync", k_sb[i][:], k_h[:, h, :], deps=dep)
                in_ld[i] = em.dma("sync", v_sb[i][:], v_h[:, h], deps=dep)
                return i

            load_pair(0)
            expS_use = [None, None]
            ps_s_use = [None, None]
            ctx_use = None
            bc_use = None
            out_dma = None

            for h in range(NPAIR):
                hb = h % 2
                if h + 1 < NPAIR:
                    load_pair(h + 1)
                pair_last = None
                for qt in range(NQT):
                    nkc = 4 * (qt + 1)
                    ctok = None
                    for kc in range(nkc):
                        i2 = kc % 2
                        s_tok = em.mm(
                            ps_s[i2][:], k_sb[hb][:, kc * 128:(kc + 1) * 128],
                            q_sb[hb][:, qt * TT:(qt + 1) * TT], True, True,
                            deps=[in_ld[hb], ps_s_use[i2]], inc=True)
                        off = kc * 128 - qt * TT
                        if off >= 0:
                            midx = off // 128
                            s_tok = em.vec("tensor_add", ps_s[i2][:],
                                           ps_s[i2][:], masks[:, midx, :],
                                           deps=[s_tok], inc=True)
                        e_tok = em.act(expS[i2][:], ps_s[i2][:], AF.Exp,
                                       deps=[s_tok, expS_use[i2]], inc=True)
                        ps_s_use[i2] = e_tok
                        ctok = em.mm(ps_ctx[:], v_sb[hb][:, kc, :],
                                     expS[i2][:], kc == 0, kc == nkc - 1,
                                     deps=([e_tok, ctx_use] if kc == 0
                                           else [e_tok]),
                                     inc=True)
                        expS_use[i2] = ctok
                    rc = em.vec("reciprocal", dn[64:65, :], ps_ctx[64:65, :],
                                deps=[ctok], inc=True)
                    bt = em.mm(ps_bc[:], dones[64:65, :], dn[64:65, :], True,
                               True, deps=[rc, bc_use], inc=True)
                    cp = em.scal("copy", bc_sb[:], ps_bc[:], deps=[bt],
                                 inc=True)
                    bc_use = cp
                    nt = em.vec("tensor_mul", ctx_sb[:], ps_ctx[0:64, :],
                                bc_sb[:], deps=[cp, out_dma], inc=True)
                    ctx_use = nt
                    pair_last = nt
                    out_dma = em.dma("sync", ctx_o[:, h, qt * TT:(qt + 1) * TT],
                                     ctx_sb[:], deps=[nt])
                in_use[hb] = pair_last

            em.emit("gpsimd", lambda e: e.nop(), deps=[em.bar("sync")])
            em.finalize(block)
    return nc


def l0_masks_img():
    masks = np.zeros((128, 4, TT), dtype=np.float32)
    i = np.arange(128)[:, None]
    j = np.arange(TT)[None, :]
    for n, off in enumerate((0, 128, 256, 384)):
        masks[:, n, :] = np.where(off + i <= j, 0.0, -1e30)
    return masks


def host_prep_p2(q_full, k_full, v_full):
    masks = l0_masks_img()
    ones = np.ones((S, 1), dtype=BF)
    in_maps = []
    for c in range(NCORES):
        qs, ks, vs = [], [], []
        for j in range(NPAIR):
            gp = c * NPAIR + j
            b, h = divmod(gp, 16)
            rs = slice(h * 64, (h + 1) * 64)
            cs = slice(b * S, (b + 1) * S)
            qs.append(q_full[rs, cs])
            ks.append(k_full[rs, cs])
            vt = np.concatenate([v_full[rs, cs].T, ones], axis=1)
            vs.append(vt.reshape(32, 128, 65).transpose(1, 0, 2))
        in_maps.append({
            "q": np.ascontiguousarray(np.stack(qs, axis=1)),
            "k": np.ascontiguousarray(np.stack(ks, axis=1)),
            "v": np.ascontiguousarray(np.stack(vs, axis=1)),
            "masks": masks,
        })
    return in_maps


def host_post_p2(results):
    ctx_full = np.empty((D, NTOK), dtype=BF)
    for c in range(NCORES):
        for j in range(NPAIR):
            gp = c * NPAIR + j
            b, h = divmod(gp, 16)
            ctx_full[h * 64:(h + 1) * 64, b * S:(b + 1) * S] = \
                results[c]["ctx"][:, j, :]
    return ctx_full



P3W = TPC + 2


def build_p3():
    nc = bass.Bass("TRN2", target_bir_lowering=False)
    ctx_h = nc.dram_tensor("ctxin", [128, NKC, P3W], BF16,
                           kind="ExternalInput").ap()
    r_h = nc.dram_tensor("rin", [128, NKC, P3W], F32, kind="ExternalInput").ap()
    wo_h = nc.dram_tensor("wo", [128, NKC * NKC * 128], F32,
                          kind="ExternalInput").ap()
    wqkv_h = nc.dram_tensor("wqkv", [128, 3 * NKC * NKC * 128], F32,
                            kind="ExternalInput").ap()
    bo_h = nc.dram_tensor("bo", [128, NKC], F32, kind="ExternalInput").ap()
    cw_h = nc.dram_tensor("cw", [128, 3 * NKC], F32, kind="ExternalInput").ap()
    cb_h = nc.dram_tensor("cb", [128, NKC], F32, kind="ExternalInput").ap()
    hl_h = nc.dram_tensor("hl", [128, 2], F32, kind="ExternalInput").ap()

    x1_o = nc.dram_tensor("x1", [128, NKC, P3W], F32, kind="ExternalOutput").ap()
    xn1_o = nc.dram_tensor("xn1", [128, NKC, P3W], F32,
                           kind="ExternalOutput").ap()
    q1_o = nc.dram_tensor("q1", [128, NKC, P3W], F32, kind="ExternalOutput").ap()
    k1_o = nc.dram_tensor("k1", [128, NKC, P3W], BF16,
                          kind="ExternalOutput").ap()
    v1_o = nc.dram_tensor("v1", [128, NKC, P3W], BF16,
                          kind="ExternalOutput").ap()
    loc_o = nc.dram_tensor("loc", [128, NKC, TPC], F32,
                           kind="ExternalOutput").ap()

    TILES = [(0, TT), (TT, TT), (2 * TT, TT), (3 * TT, TT), (4 * TT, 2)]

    with contextlib.ExitStack() as ctx:
        em = Em(nc)
        em.init_sems(ctx)
        sb = lambda name, shape, dt: ctx.enter_context(
            nc.sbuf_tensor(name, shape, dt))
        psa = lambda name, shape, dt=F32: ctx.enter_context(
            nc.psum_tensor(name, shape, dt))

        ctx_t = [sb("ctx0", [128, NKC, TT], BF16)]
        r_t = [sb("r0", [128, NKC, TT], F32)]
        x1_t = sb("x1t", [128, NKC, TT], F32)
        xn1 = sb("xn1s", [128, NKC, P3W], F32)
        xnb = sb("xnb", [128, NKC, TT], BF16)
        q1_t = sb("q1t", [128, NKC, TT], F32)
        k1_t = sb("k1t", [128, NKC, TT], BF16)
        v1_t = sb("v1t", [128, NKC, TT], BF16)
        loc_t = sb("loct", [128, NKC, TT], F32)
        sqbuf = [sb(f"sq{i}", [128, TT], F32) for i in range(2)]
        ones128 = sb("ones128", [128, 1], F32)
        ones1r = sb("ones1r", [1, 128], F32)
        eps_col = sb("eps_col", [1, 1], F32)
        xn_pad = sb("xn_pad", [128, NKC, 128], F32)
        rowbuf = sb("rowbuf", [1, 6 * TT], F32)
        tmp1 = sb("tmp1", [128, TT], F32)
        tmp2 = sb("tmp2", [128, TT], F32)
        tmp3 = sb("tmp3", [128, TT], F32)
        bo_c = sb("bo_c", [128, NKC], F32)
        cw_c = sb("cw_c", [128, 3 * NKC], F32)
        cb_c = sb("cb_c", [128, NKC], F32)
        hl_c = sb("hl_c", [128, 2], F32)
        wbuf = WBuf(em, [sb(f"wb{i}", [128, NKC, 128], BF16) for i in range(2)])

        ps_mu = psa("ps_mu", [1, TT])
        ps_var = psa("ps_var", [1, TT])
        ps_brstd = psa("ps_brstd", [128, TT])
        ps_bmur = psa("ps_bmur", [128, TT])
        pool = PsPool([psa(f"ps_main{i}", [128, TT]) for i in range(3)])

        def wo_slice(oc):
            off = oc * (NKC * 128)
            return wo_h[:, off:off + NKC * 128].rearrange(
                "p (c n) -> p c n", c=NKC)

        def wqkv_slice(gi):
            off = gi * (NKC * 128)
            return wqkv_h[:, off:off + NKC * 128].rearrange(
                "p (c n) -> p c n", c=NKC)

        with nc.Block() as block:
            em.dma("sync", bo_c[:], bo_h[:])
            em.dma("sync", cw_c[:], cw_h[:])
            em.dma("sync", cb_c[:], cb_h[:])
            cld = em.dma("sync", hl_c[:], hl_h[:])
            em.emit("vector", lambda e: e.memset(ones128[:], 1.0), inc=1)
            em.emit("vector", lambda e: e.memset(ones1r[:], 1.0), inc=1)
            em.emit("vector", lambda e: e.memset(eps_col[:], 1e-5), inc=1)
            consts = em.bar("vector")
            em.emit("tensor", lambda e: e.nop(), deps=[cld, consts])
            em.emit("scalar", lambda e: e.nop(), deps=[cld, consts])
            em.emit("vector", lambda e: e.nop(), deps=[cld])

            state = {"tmp128": tmp1, "mu_free": consts, "var_free": consts,
                     "b_free": consts}
            in_use = [None, None]
            out_done = None
            xn_done = []
            x1_free = []
            wbuf.load(wo_slice(0))

            for ti, (off, T) in enumerate(TILES):
                ib = 0
                csl = slice(off, off + T)
                ld1 = em.dma("sync", ctx_t[ib][:, :, :T], ctx_h[:, :, csl],
                             deps=[in_use[ib]])
                ld2 = em.dma("sync", r_t[ib][:, :, :T], r_h[:, :, csl],
                             deps=[in_use[ib]])
                x1_done = None
                for oc in range(NKC):
                    bi = (wbuf.i - 1) % 2
                    cur = wbuf.tok(bi)
                    pi, pm, pdep = pool.next()
                    last = None
                    for kc in range(NKC):
                        last = em.mm(pm[:, :T], wbuf.buf(bi)[:, kc, :],
                                     ctx_t[ib][:, kc, :T], kc == 0,
                                     kc == NKC - 1,
                                     deps=([cur, ld1, pdep] if kc == 0 else []),
                                     inc=(kc == NKC - 1))
                    wbuf.used(bi, last)
                    ng = oc + 1
                    if ng < NKC:
                        wbuf.load(wo_slice(ng))
                    else:
                        wbuf.load(wqkv_slice(0))
                    x1_done = em.vec("scalar_tensor_tensor", x1_t[:, oc, :T],
                                     pm[:, :T], bo_c[:, oc:oc + 1],
                                     r_t[ib][:, oc, :T], ALU.add, ALU.add,
                                     deps=[last, ld2, out_done] + x1_free,
                                     inc=True)
                    pool.done(pi, x1_done)
                x1_dma = em.dma("sync", x1_o[:, :, csl], x1_t[:, :, :T],
                                deps=[em.bar("vector")])
                state["xn_free"] = xn_done or None
                x_chunks_w = [x1_t[:, kc, :max(T, 128)] for kc in range(NKC)]
                xn_tok = emit_ln(em, ps_mu, ps_var, ps_brstd, ps_bmur, ones128,
                                 ones1r, rowbuf, sqbuf, eps_col, x_chunks_w, T,
                                 lambda kc, csl=csl: xn1[:, kc, csl],
                                 em.bar("vector"), state, xn_pad=xn_pad)
                xb_tok = None
                for kc in range(NKC):
                    xb_tok = em.vec("tensor_copy", xnb[:, kc, :T],
                                    xn1[:, kc, csl], deps=[xn_tok],
                                    inc=(kc == NKC - 1))
                xn1_dma = em.dma("sync", xn1_o[:, :, csl], xn1[:, :, csl],
                                 deps=[xn_tok])
                qkv_last = None
                for gi in range(3 * NKC):
                    proj, oc = divmod(gi, NKC)
                    bi = (wbuf.i - 1) % 2
                    cur = wbuf.tok(bi)
                    pi, pm, pdep = pool.next()
                    last = None
                    for kc in range(NKC):
                        last = em.mm(pm[:, :T], wbuf.buf(bi)[:, kc, :],
                                     xnb[:, kc, :T], kc == 0, kc == NKC - 1,
                                     deps=([cur, xb_tok, pdep]
                                           if kc == 0 else []),
                                     inc=(kc == NKC - 1))
                    wbuf.used(bi, last)
                    ng = gi + 1
                    if ng < 3 * NKC:
                        wbuf.load(wqkv_slice(ng))
                    elif ti < len(TILES) - 1:
                        wbuf.load(wo_slice(0))
                    if proj == 2:
                        t = em.act(v1_t[:, oc, :T], pm[:, :T], AF.Identity,
                                   deps=[last, out_done], inc=True)
                        pool.done(pi, t)
                        qkv_last = t
                    else:
                        mn = em.vec("tensor_scalar_min", tmp2[:, :T],
                                    pm[:, :T], 0.0, deps=[last], inc=True)
                        et = em.act(tmp3[:, :T], tmp2[:, :T], AF.Exp,
                                    deps=[mn], inc=True)
                        em.vec("tensor_relu", tmp2[:, :T], pm[:, :T],
                               deps=[et])
                        dst = q1_t if proj == 0 else k1_t
                        t = em.vec("tensor_add", dst[:, oc, :T], tmp2[:, :T],
                                   tmp3[:, :T], deps=[et, out_done], inc=True)
                        pool.done(pi, t)
                        qkv_last = t
                q1_dma = em.dma("sync", q1_o[:, :, csl], q1_t[:, :, :T],
                                deps=[em.bar("vector")])
                k1_dma = em.dma("sync", k1_o[:, :, csl], k1_t[:, :, :T],
                                deps=[em.bar("vector")])
                v1_dma = em.dma("sync", v1_o[:, :, csl], v1_t[:, :, :T],
                                deps=[em.bar("scalar"), em.bar("vector")])
                out_done = v1_dma
                in_use[ib] = em.bar("vector")
                xn_done = [em.bar("tensor"), xn1_dma]
                x1_free = [em.bar("tensor"), em.bar("scalar"), x1_dma]

            hz = None
            for kc in range(NKC):
                em.vec("tensor_mul", xn1[:, kc, 0:1], xn1[:, kc, 0:1],
                       hl_c[:, 0:1], deps=xn_done)
                hz = em.vec("tensor_mul", xn1[:, kc, P3W - 1:P3W],
                            xn1[:, kc, P3W - 1:P3W], hl_c[:, 1:2],
                            inc=(kc == NKC - 1))
            conv_done = None
            for j in range(NTT):
                o = 1 + j * TT
                ct = None
                for kc in range(NKC):
                    em.vec("tensor_scalar", tmp1[:], xn1[:, kc, o - 1:o - 1 + TT],
                           cw_c[:, 3 * kc:3 * kc + 1], None, ALU.mult,
                           deps=[hz])
                    em.vec("scalar_tensor_tensor", tmp1[:], xn1[:, kc, o:o + TT],
                           cw_c[:, 3 * kc + 1:3 * kc + 2], tmp1[:],
                           ALU.mult, ALU.add)
                    em.vec("scalar_tensor_tensor", tmp1[:],
                           xn1[:, kc, o + 1:o + 1 + TT],
                           cw_c[:, 3 * kc + 2:3 * kc + 3], tmp1[:],
                           ALU.mult, ALU.add)
                    ct = em.vec("tensor_scalar", loc_t[:, kc, :], tmp1[:],
                                cb_c[:, kc:kc + 1], None, ALU.add,
                                deps=[conv_done], inc=True)
                conv_done = em.dma("sync", loc_o[:, :, j * TT:(j + 1) * TT],
                                   loc_t[:], deps=[ct])

            em.emit("gpsimd", lambda e: e.nop(), deps=[em.bar("sync")])
            em.finalize(block)
    return nc


def host_prep_p3(inputs, ctx_full, r_full):
    blocks128 = [(i * 128, (i + 1) * 128) for i in range(NKC)]
    wo_img = tile_w(np.ascontiguousarray(np.asarray(inputs["L0_attn_Wo"]).T),
                    blocks128)
    wq = np.ascontiguousarray(np.asarray(inputs["L1_attn_Wq"]).T)
    wk = np.ascontiguousarray(np.asarray(inputs["L1_attn_Wk"]).T)
    wv = np.ascontiguousarray(np.asarray(inputs["L1_attn_Wv"]).T)
    wqkv_img = np.concatenate([tile_w(w, blocks128) for w in (wq, wk, wv)],
                              axis=1)
    bo = col_img(np.asarray(inputs["L0_attn_bo"]))
    cw = np.asarray(inputs["L1_attn_conv_w"]).astype(np.float32)
    cw_img = np.ascontiguousarray(
        cw.reshape(NKC, 128, 3).transpose(1, 0, 2).reshape(128, 3 * NKC))
    cb = col_img(np.asarray(inputs["L1_attn_conv_b"]))

    def with_halo(full, c, dtype):
        sl = np.zeros((D, P3W), dtype=dtype)
        lo = c * TPC
        hi = lo + TPC
        sl[:, 1:1 + TPC] = full[:, lo:hi]
        if c % 2 == 1:
            sl[:, 0] = full[:, lo - 1]
        if c % 2 == 0:
            sl[:, -1] = full[:, hi]
        return to_fm(sl)

    in_maps = []
    for c in range(NCORES):
        hl = np.zeros((128, 2), np.float32)
        hl[:, 0] = 1.0 if c % 2 == 1 else 0.0
        hl[:, 1] = 1.0 if c % 2 == 0 else 0.0
        in_maps.append({
            "ctxin": with_halo(ctx_full, c, BF),
            "rin": with_halo(r_full.astype(np.float32), c, np.float32),
            "wo": wo_img, "wqkv": wqkv_img, "bo": bo,
            "cw": cw_img, "cb": cb, "hl": hl,
        })
    return in_maps



def build_p4():
    nc = bass.Bass("TRN2", target_bir_lowering=False)
    kT_h = nc.dram_tensor("kT", [128, NPAIR, 32, 64], BF16,
                          kind="ExternalInput").ap()
    vT_h = nc.dram_tensor("vT", [128, NPAIR, 32, 65], BF16,
                          kind="ExternalInput").ap()
    q_h = nc.dram_tensor("q", [64, NPAIR, S], F32, kind="ExternalInput").ap()
    at_o = nc.dram_tensor("at", [64, NPAIR, S], BF16,
                          kind="ExternalOutput").ap()

    with contextlib.ExitStack() as ctx:
        em = Em(nc)
        em.init_sems(ctx)
        sb = lambda name, shape, dt: ctx.enter_context(
            nc.sbuf_tensor(name, shape, dt))
        psa = lambda name, shape, dt=F32: ctx.enter_context(
            nc.psum_tensor(name, shape, dt))

        k_sb = [sb(f"k{i}", [128, 32, 64], BF16) for i in range(2)]
        v_sb = [sb(f"v{i}", [128, 32, 65], BF16) for i in range(2)]
        q_sb = [sb(f"q{i}", [64, S], F32) for i in range(2)]
        kv_sb = sb("kv", [64, 65], F32)
        dn = sb("dn", [65, TT], F32)
        dones = sb("dones", [65, 64], F32)
        bc_sb = sb("bc_sb", [64, TT], F32)
        at_sb = sb("at_sb", [64, TT], BF16)

        ps_kv = psa("ps_kv", [64, 65])
        ps_at = psa("ps_at", [65, TT])
        ps_bc = psa("ps_bc", [64, TT])

        with nc.Block() as block:
            em.emit("vector", lambda e: e.memset(dones[:], 1.0), inc=1)
            consts = em.bar("vector")
            em.emit("tensor", lambda e: e.nop(), deps=[consts])

            in_use = [None, None]
            in_ld = [None, None]

            def load_pair(h):
                i = h % 2
                dep = [in_use[i]] if in_use[i] else []
                em.dma("sync", k_sb[i][:], kT_h[:, h], deps=dep)
                em.dma("sync", v_sb[i][:], vT_h[:, h], deps=dep)
                in_ld[i] = em.dma("sync", q_sb[i][:], q_h[:, h, :], deps=dep)
                return i

            load_pair(0)
            kv_use = None
            at_use = None
            bc_use = None
            out_dma = None

            for h in range(NPAIR):
                hb = h % 2
                if h + 1 < NPAIR:
                    load_pair(h + 1)
                last = None
                for kc in range(32):
                    last = em.mm(ps_kv[:], k_sb[hb][:, kc, :],
                                 v_sb[hb][:, kc, :], kc == 0, kc == 31,
                                 deps=([in_ld[hb], kv_use] if kc == 0 else []),
                                 inc=(kc == 31))
                cp = em.scal("copy", kv_sb[:], ps_kv[:], deps=[last], inc=True)
                kv_use = cp
                pair_last = None
                for qt in range(NQT):
                    att = em.mm(ps_at[:], kv_sb[:],
                                q_sb[hb][:, qt * TT:(qt + 1) * TT], True, True,
                                deps=[cp, at_use], inc=True)
                    em.vec("tensor_scalar_max", dn[64:65, :], ps_at[64:65, :],
                           1e-6, deps=[att])
                    rc = em.vec("reciprocal", dn[64:65, :], dn[64:65, :],
                                inc=True)
                    bt = em.mm(ps_bc[:], dones[64:65, :], dn[64:65, :], True,
                               True, deps=[rc, bc_use], inc=True)
                    cpb = em.scal("copy", bc_sb[:], ps_bc[:], deps=[bt],
                                  inc=True)
                    bc_use = cpb
                    nt = em.vec("tensor_mul", at_sb[:], ps_at[0:64, :],
                                bc_sb[:], deps=[cpb, out_dma], inc=True)
                    at_use = nt
                    pair_last = nt
                    out_dma = em.dma("sync", at_o[:, h, qt * TT:(qt + 1) * TT],
                                     at_sb[:], deps=[nt])
                in_use[hb] = pair_last

            em.emit("gpsimd", lambda e: e.nop(), deps=[em.bar("sync")])
            em.finalize(block)
    return nc


def host_prep_p4(q1_full, k1_full, v1_full):
    ones = np.ones((S, 1), dtype=BF)
    in_maps = []
    for c in range(NCORES):
        ks, vs, qs = [], [], []
        for j in range(NPAIR):
            gp = c * NPAIR + j
            b, h = divmod(gp, 16)
            rs = slice(h * 64, (h + 1) * 64)
            cs = slice(b * S, (b + 1) * S)
            kt = np.ascontiguousarray(k1_full[rs, cs].T).reshape(
                32, 128, 64).transpose(1, 0, 2)
            vt = np.concatenate(
                [np.ascontiguousarray(v1_full[rs, cs].T), ones], axis=1)
            vs.append(vt.reshape(32, 128, 65).transpose(1, 0, 2))
            ks.append(kt)
            qs.append(q1_full[rs, cs])
        in_maps.append({
            "kT": np.ascontiguousarray(np.stack(ks, axis=1)),
            "vT": np.ascontiguousarray(np.stack(vs, axis=1)),
            "q": np.ascontiguousarray(np.stack(qs, axis=1)),
        })
    return in_maps


def host_post_p4(results):
    at_full = np.empty((D, NTOK), dtype=BF)
    for c in range(NCORES):
        for j in range(NPAIR):
            gp = c * NPAIR + j
            b, h = divmod(gp, 16)
            at_full[h * 64:(h + 1) * 64, b * S:(b + 1) * S] = \
                results[c]["at"][:, j, :]
    return at_full



def build_p5():
    nc = bass.Bass("TRN2", target_bir_lowering=False)
    x1_h = nc.dram_tensor("x1", [128, NKC, TPC], F32, kind="ExternalInput").ap()
    at_h = nc.dram_tensor("at", [128, NKC, TPC], BF16,
                          kind="ExternalInput").ap()
    loc_h = nc.dram_tensor("loc", [128, NKC, TPC], F32,
                           kind="ExternalInput").ap()
    wp_h = nc.dram_tensor("wp", [128, NKC * NKC * 128], F32,
                          kind="ExternalInput").ap()
    bp_h = nc.dram_tensor("bp", [128, NKC], F32, kind="ExternalInput").ap()
    o_h = nc.dram_tensor("x1p", [128, NKC, TPC], F32, kind="ExternalOutput").ap()

    with contextlib.ExitStack() as ctx:
        em = Em(nc)
        em.init_sems(ctx)
        sb = lambda name, shape, dt: ctx.enter_context(
            nc.sbuf_tensor(name, shape, dt))
        psa = lambda name, shape, dt=F32: ctx.enter_context(
            nc.psum_tensor(name, shape, dt))

        x1_t = [sb(f"x1{i}", [128, NKC, TT], F32) for i in range(2)]
        at_t = [sb(f"at{i}", [128, NKC, TT], BF16) for i in range(2)]
        loc_t = [sb(f"loc{i}", [128, NKC, TT], F32) for i in range(2)]
        u_t = sb("u_t", [128, NKC, TT], BF16)
        o_t = sb("o_t", [128, NKC, TT], F32)
        bp_c = sb("bp_c", [128, NKC], F32)
        wbuf = WBuf(em, [sb(f"wb{i}", [128, NKC, 128], BF16) for i in range(2)])
        pool = PsPool([psa(f"ps{i}", [128, TT]) for i in range(3)])

        def wp_slice(oc):
            off = oc * (NKC * 128)
            return wp_h[:, off:off + NKC * 128].rearrange(
                "p (c n) -> p c n", c=NKC)

        with nc.Block() as block:
            cld = em.dma("sync", bp_c[:], bp_h[:])
            em.emit("vector", lambda e: e.nop(), deps=[cld])
            in_use = [None, None]
            out_dma = None
            u_free = []
            wbuf.load(wp_slice(0))
            for ti in range(NTT):
                ib = ti % 2
                csl = slice(ti * TT, (ti + 1) * TT)
                l1 = em.dma("sync", x1_t[ib][:], x1_h[:, :, csl],
                            deps=[in_use[ib]])
                l2 = em.dma("sync", at_t[ib][:], at_h[:, :, csl],
                            deps=[in_use[ib]])
                l3 = em.dma("sync", loc_t[ib][:], loc_h[:, :, csl],
                            deps=[in_use[ib]])
                ut = None
                for kc in range(NKC):
                    ut = em.vec("tensor_add", u_t[:, kc, :], at_t[ib][:, kc, :],
                                loc_t[ib][:, kc, :],
                                deps=[l2, l3] + u_free,
                                inc=(kc == NKC - 1))
                o_done = None
                for oc in range(NKC):
                    bi = (wbuf.i - 1) % 2
                    cur = wbuf.tok(bi)
                    pi, pm, pdep = pool.next()
                    last = None
                    for kc in range(NKC):
                        last = em.mm(pm[:], wbuf.buf(bi)[:, kc, :],
                                     u_t[:, kc, :], kc == 0, kc == NKC - 1,
                                     deps=([cur, ut, pdep] if kc == 0 else []),
                                     inc=(kc == NKC - 1))
                    wbuf.used(bi, last)
                    wbuf.load(wp_slice((oc + 1) % NKC))
                    o_done = em.vec("scalar_tensor_tensor", o_t[:, oc, :],
                                    pm[:], bp_c[:, oc:oc + 1],
                                    x1_t[ib][:, oc, :], ALU.add, ALU.add,
                                    deps=[last, l1, out_dma], inc=True)
                    pool.done(pi, o_done)
                out_dma = em.dma("sync", o_h[:, :, csl], o_t[:],
                                 deps=[em.bar("vector")])
                in_use[ib] = em.bar("vector")
                u_free = [em.bar("tensor")]
            em.emit("gpsimd", lambda e: e.nop(), deps=[em.bar("sync")])
            em.finalize(block)
    return nc


def fm_p3(img):
    P, C, W = img.shape
    return np.ascontiguousarray(img.transpose(1, 0, 2).reshape(C * P, W))


def kernel(**inputs):
    cores = list(range(NCORES))
    res1 = run_bass_kernel_spmd(build_p1(), host_prep_p1(inputs),
                                core_ids=cores)
    q_full = np.concatenate(
        [untile_out(res1.results[c]["q"]) for c in cores], axis=1)
    k_full = np.concatenate(
        [untile_out(res1.results[c]["k"]) for c in cores], axis=1)
    v_full = np.concatenate(
        [untile_out(res1.results[c]["v"]) for c in cores], axis=1)
    r_full = np.concatenate(
        [untile_out(res1.results[c]["r"]) for c in cores], axis=1)
    res2 = run_bass_kernel_spmd(build_p2(),
                                host_prep_p2(q_full, k_full, v_full),
                                core_ids=cores)
    ctx_full = host_post_p2([r for r in res2.results])
    res3 = run_bass_kernel_spmd(build_p3(),
                                host_prep_p3(inputs, ctx_full, r_full),
                                core_ids=cores)
    own = slice(1, 1 + TPC)
    x1_full = np.concatenate(
        [fm_p3(res3.results[c]["x1"])[:, own] for c in cores], axis=1)
    xn1_full = np.concatenate(
        [fm_p3(res3.results[c]["xn1"])[:, own] for c in cores], axis=1)
    q1_full = np.concatenate(
        [fm_p3(res3.results[c]["q1"])[:, own] for c in cores], axis=1)
    k1_full = np.concatenate(
        [fm_p3(res3.results[c]["k1"])[:, own] for c in cores], axis=1)
    v1_full = np.concatenate(
        [fm_p3(res3.results[c]["v1"])[:, own] for c in cores], axis=1)
    loc_full = np.concatenate(
        [fm_p3(res3.results[c]["loc"]) for c in cores], axis=1)
    res4 = run_bass_kernel_spmd(build_p4(),
                                host_prep_p4(q1_full, k1_full, v1_full),
                                core_ids=cores)
    at_full = host_post_p4([r for r in res4.results])
    wp_img = tile_w(np.ascontiguousarray(np.asarray(inputs["L1_attn_Wp"]).T),
                    [(i * 128, (i + 1) * 128) for i in range(NKC)])
    bp = col_img(np.asarray(inputs["L1_attn_bp"]))
    in_maps5 = []
    for c in cores:
        sl = slice(c * TPC, (c + 1) * TPC)
        in_maps5.append({
            "x1": to_fm(x1_full[:, sl]),
            "at": to_fm(at_full[:, sl]),
            "loc": to_fm(loc_full[:, sl]),
            "wp": wp_img, "bp": bp,
        })
    res5 = run_bass_kernel_spmd(build_p5(), in_maps5, core_ids=cores)
    x1p_full = np.concatenate(
        [fm_p3(res5.results[c]["x1p"]) for c in cores], axis=1)
    nc6 = build_p6()
    nc7 = build_p7()
    nc7f = build_p7(final_add=True)
    y = xn1_full
    for i in range(6):
        res6 = run_bass_kernel_spmd(nc6, host_prep_p6(inputs, i, y),
                                    core_ids=cores)
        yp = np.concatenate(
            [fm_p3(res6.results[c]["yo"]) for c in cores], axis=1)
        if i < 5:
            res7 = run_bass_kernel_spmd(nc7, host_prep_p7(inputs, i, yp),
                                        core_ids=cores)
        else:
            res7 = run_bass_kernel_spmd(
                nc7f, host_prep_p7(inputs, i, yp, xe_full=x1p_full),
                core_ids=cores)
        y = np.concatenate(
            [fm_p3(res7.results[c]["yo"]) for c in cores], axis=1)
    return np.ascontiguousarray(y.T).reshape(B, S, D)



P6W = TPC + 258
NQB = TPC // 256
QB = 256


def build_p6():
    nc = bass.Bass("TRN2", target_bir_lowering=False)
    y_h = nc.dram_tensor("y", [128, NKC, P6W], F32, kind="ExternalInput").ap()
    wqkv_h = nc.dram_tensor("wqkv", [128, 3 * NKC * NKC * 128], F32,
                            kind="ExternalInput").ap()
    wo_h = nc.dram_tensor("wo", [128, NKC * NKC * 128], F32,
                          kind="ExternalInput").ap()
    cSq_h = nc.dram_tensor("cSq", [128, TPC], BF16, kind="ExternalInput").ap()
    sSq_h = nc.dram_tensor("sSq", [128, TPC], BF16, kind="ExternalInput").ap()
    cIk_h = nc.dram_tensor("cIk", [128, QB + TPC], BF16,
                           kind="ExternalInput").ap()
    sIk_h = nc.dram_tensor("sIk", [128, QB + TPC], BF16,
                           kind="ExternalInput").ap()
    qs_h = nc.dram_tensor("qs", [128, 4], F32, kind="ExternalInput").ap()
    mc_h = nc.dram_tensor("mconst", [128, 4, QB], BF16,
                          kind="ExternalInput").ap()
    mf_h = nc.dram_tensor("mfirst", [128, 2, QB], BF16,
                          kind="ExternalInput").ap()
    hsw_h = nc.dram_tensor("hsw", [128, 128], BF16, kind="ExternalInput").ap()
    hdi_h = nc.dram_tensor("hdi", [128, 2], F32, kind="ExternalInput").ap()
    hdi2_h = nc.dram_tensor("hdi2", [2, 128], F32, kind="ExternalInput").ap()
    yo_h = nc.dram_tensor("yo", [128, NKC, TPC], F32,
                          kind="ExternalOutput").ap()

    with contextlib.ExitStack() as ctx:
        em = Em(nc)
        em.init_sems(ctx)
        sb = lambda name, shape, dt: ctx.enter_context(
            nc.sbuf_tensor(name, shape, dt))
        psa = lambda name, shape, dt=F32: ctx.enter_context(
            nc.psum_tensor(name, shape, dt))

        y_own = sb("y_own", [128, NKC, TPC], F32)
        yb = sb("yb", [128, NKC, P6W], BF16)
        qp = sb("qp", [128, TPC], BF16)
        kp = sb("kp", [128, QB + TPC], BF16)
        v_tm = sb("v_tm", [128, 18, 128], BF16)
        ctx_all = sb("ctx_all", [128, NKC, TPC], BF16)
        yout = sb("yout", [128, NKC, TT], F32)
        cSq = sb("cSq_s", [128, TPC], BF16)
        sSq = sb("sSq_s", [128, TPC], BF16)
        cIk = sb("cIk_s", [128, QB + TPC], BF16)
        sIk = sb("sIk_s", [128, QB + TPC], BF16)
        qs_c = sb("qs_c", [128, 4], F32)
        mconst = sb("mconst_s", [128, 4, QB], BF16)
        mfirst = sb("mfirst_s", [128, 2, QB], BF16)
        hsw = sb("hsw_s", [128, 128], BF16)
        hdi = sb("hdi_s", [128, 2], F32)
        hdi2 = sb("hdi2_s", [2, 128], F32)
        ones1x64 = sb("ones1x64", [1, 64], F32)
        onescol_b = sb("onescol_b", [128, 1], BF16)
        q_bf = sb("q_bf", [128, TT], BF16)
        sw_sb = sb("sw_sb", [128, TT], BF16)
        sq_sb = sb("sq_sb", [128, TT], F32)
        rows = sb("rows", [2, 2 * TT], F32)
        eps24 = sb("eps24", [2, 1], F32)
        tmp_a = sb("tmp_a", [128, TT], F32)
        tmp_b = sb("tmp_b", [128, TT], F32)
        expS = [sb(f"expS{i}", [128, QB], BF16) for i in range(2)]
        dn_sb = sb("dn_sb", [1, 2 * QB], F32)
        bc_sb = sb("bc_sb", [128, QB], F32)
        wqkv_b = WBuf(em, [sb(f"wq{i}", [128, NKC, 128], BF16)
                           for i in range(2)])
        wo_b = WBuf(em, [sb(f"wo{i}", [128, NKC, 128], BF16)
                         for i in range(2)])

        pool = PsPool([psa(f"pp{i}", [128, TT]) for i in range(4)])
        ps_dn = psa("ps_dn", [1, TT])
        ps_s = [psa(f"ps_s{i}", [128, QB]) for i in range(2)]
        ps_full = psa("ps_full", [128, QB])

        def wqkv_slice(proj, g):
            off = (proj * NKC + g) * (NKC * 128)
            return wqkv_h[:, off:off + NKC * 128].rearrange(
                "p (c n) -> p c n", c=NKC)

        def wo_slice(oc):
            off = oc * (NKC * 128)
            return wo_h[:, off:off + NKC * 128].rearrange(
                "p (c n) -> p c n", c=NKC)

        def yb_sh(kc, w0, T):
            return yb[:, kc, (w0 if kc < 4 else w0 - 1):][:, :T]

        with nc.Block() as block:
            for t, hh in ((cSq, cSq_h), (sSq, sSq_h), (cIk, cIk_h),
                          (sIk, sIk_h), (qs_c, qs_h), (mconst, mc_h),
                          (mfirst, mf_h), (hsw, hsw_h), (hdi, hdi_h),
                          (hdi2, hdi2_h)):
                cld = em.dma("sync", t[:], hh[:])
            yld = em.dma("sync", y_own[:], y_h[:, :, 258:])
            ybld = em.dma("gpsimd", yb[:], y_h[:])
            em.emit("vector", lambda e: e.memset(ones1x64[:], 1.0), inc=1)
            em.emit("vector", lambda e: e.memset(onescol_b[:], 1.0), inc=1)
            cv = em.emit("vector", lambda e: e.memset(eps24[:], 1e-24), inc=1)
            consts = em.bar("vector")
            em.emit("tensor", lambda e: e.nop(), deps=[cld, ybld, consts])
            em.emit("scalar", lambda e: e.nop(), deps=[cld, ybld, consts])
            em.emit("vector", lambda e: e.nop(), deps=[cld, ybld])

            wqkv_b.load(wqkv_slice(0, 0))
            qp_use = None
            kp_use = None
            vtm_use = None
            ctx_use = None
            expS_use = [None, None]
            ps_s_use = [None, None]
            ps_full_use = None
            ps_dn_use = None
            yo_dma = None

            n2_r = rows[:, 0:TT]
            al_r = rows[:, TT:2 * TT]

            rp = {}

            def rope_path(dst, dst_off, T, w0, qsi, qsri, ctab, stab, tab_off,
                          wtok, in_dep):
                pi, pm, pdep = pool.next()
                last = None
                for kc in range(NKC):
                    last = em.mm(pm[:, :T], wqkv_b.buf(wtok)[:, kc, :],
                                 yb_sh(kc, w0, T), kc == 0, kc == NKC - 1,
                                 deps=([wqkv_b.tok(wtok), pdep] + in_dep
                                       if kc == 0 else []),
                                 inc=(kc == NKC - 1))
                ev = em.act(q_bf[:, :T], pm[:, :T], AF.Identity,
                            deps=[last, rp.get("qbf_use")], inc=True)
                sqt = em.act(sq_sb[:, :T], pm[:, :T], AF.Square,
                             deps=[rp.get("sq_use")], inc=True)
                pool.done(pi, sqt)
                pi2, pm2, pdep2 = pool.next()
                n2t = em.mm(pm2[0:2, :T], hdi[:], sq_sb[:, :T], True, True,
                            deps=[sqt, pdep2], inc=True)
                rp["sq_use"] = n2t
                em.act(n2_r[:, :T], pm2[0:2, :T], AF.Ln, bias=eps24[:],
                       deps=[n2t, rp.get("al_use")])
                al = em.act(al_r[:, :T], n2_r[:, :T], AF.Exp, scale=-0.5,
                            inc=True)
                pool.done(pi2, al)
                pi3, pm3, pdep3 = pool.next()
                swt = em.mm(pm3[:, :T], hsw[:], q_bf[:, :T], True, True,
                            deps=[ev, pdep3], inc=True)
                swe = em.act(sw_sb[:, :T], pm3[:, :T], AF.Identity,
                             deps=[swt, rp.get("sw_use")], inc=True)
                pool.done(pi3, swe)
                pi4, pm4, pdep4 = pool.next()
                ab = em.mm(pm4[:, :T], hdi2[:], al_r[:, :T], True, True,
                           deps=[al, pdep4], inc=True)
                rp["al_use"] = ab
                em.vec("scalar_tensor_tensor", tmp_a[:, :T], q_bf[:, :T],
                       qs_c[:, qsi:qsi + 1], pm4[:, :T], ALU.mult, ALU.mult,
                       deps=[ab, ev])
                c1 = em.vec("scalar_tensor_tensor", tmp_b[:, :T], sw_sb[:, :T],
                            qs_c[:, qsri:qsri + 1], pm4[:, :T], ALU.mult,
                            ALU.mult, deps=[swe], inc=True)
                pool.done(pi4, c1)
                rp["qbf_use"] = c1
                rp["sw_use"] = c1
                em.vec("tensor_mul", tmp_a[:, :T], tmp_a[:, :T],
                       ctab[:, tab_off:tab_off + T])
                em.vec("tensor_mul", tmp_b[:, :T], tmp_b[:, :T],
                       stab[:, tab_off:tab_off + T])
                return em.vec("tensor_add", dst[:, dst_off:dst_off + T],
                              tmp_a[:, :T], tmp_b[:, :T], inc=True)

            for g in range(NKC):
                q_done = None
                wi = (wqkv_b.i - 1) % 2
                for t4 in range(4):
                    q_done = rope_path(qp, t4 * TT, TT, 258 + t4 * TT, 0, 1,
                                       cSq, sSq, t4 * TT, wi,
                                       [qp_use])
                wqkv_b.used(wi, q_done)
                wqkv_b.load(wqkv_slice(1, g))
                k_done = None
                wi = (wqkv_b.i - 1) % 2
                for t5, (o5, T5) in enumerate(
                        [(0, TT), (TT, TT), (2 * TT, TT), (3 * TT, TT),
                         (4 * TT, QB)]):
                    k_done = rope_path(kp, o5, T5, 2 + o5, 2, 3,
                                       cIk, sIk, o5, wi, [kp_use])
                wqkv_b.used(wi, k_done)
                wqkv_b.load(wqkv_slice(2, g))
                wi = (wqkv_b.i - 1) % 2
                v_done = None
                for c in range(18):
                    pi, pm, pdep = pool.next()
                    last = None
                    for kc in range(NKC):
                        last = em.mm(pm[:, :128],
                                     yb_sh(kc, 2 + c * 128, 128),
                                     wqkv_b.buf(wi)[:, kc, :],
                                     kc == 0, kc == NKC - 1,
                                     deps=([wqkv_b.tok(wi), pdep, vtm_use]
                                           if kc == 0 else []),
                                     inc=(kc == NKC - 1))
                    e1 = em.act(v_tm[:, c, 0:64], pm[:, 0:64], AF.Identity,
                                deps=[last], inc=True)
                    e2 = em.act(v_tm[:, c, 64:128], pm[:, 64:128],
                                AF.Identity, inc=True)
                    pool.done(pi, e2)
                    v_done = e2
                wqkv_b.used(wi, v_done)
                if g < NKC - 1:
                    wqkv_b.load(wqkv_slice(0, g + 1))

                g_last = None
                for qb in range(NQB):
                    for hd in range(2):
                        hb = slice(hd * 64, hd * 64 + 64)
                        ctok = None
                        dtok = None
                        for kchunk in range(4):
                            i2 = kchunk % 2
                            kl = qb * QB + kchunk * 128
                            st = em.mm(ps_s[i2][:],
                                       kp[hb, kl:kl + 128],
                                       qp[hb, qb * QB:(qb + 1) * QB],
                                       True, True,
                                       deps=[k_done, q_done, ps_s_use[i2]],
                                       inc=True)
                            msrc = (mfirst[:, kchunk, :]
                                    if (qb == 0 and kchunk < 2)
                                    else mconst[:, kchunk, :])
                            mt = em.vec("tensor_add", ps_s[i2][:], ps_s[i2][:],
                                        msrc, deps=[st], inc=True)
                            et = em.act(expS[i2][:], ps_s[i2][:], AF.Exp,
                                        deps=[mt, expS_use[i2]], inc=True)
                            ps_s_use[i2] = et
                            vchunk = 2 * qb + kchunk
                            ctok = em.mm(
                                ps_full[hd * 64:hd * 64 + 64, :],
                                v_tm[:, vchunk, hd * 64:hd * 64 + 64],
                                expS[i2][:], kchunk == 0, kchunk == 3,
                                deps=([et, v_done, ps_full_use]
                                      if kchunk == 0 else [et]), inc=True)
                            dtok = em.mm(
                                ps_dn[:, hd * QB:(hd + 1) * QB],
                                onescol_b[:], expS[i2][:],
                                kchunk == 0, kchunk == 3,
                                deps=([ps_dn_use] if kchunk == 0 else []),
                                inc=True)
                            expS_use[i2] = dtok
                        rc = em.vec("reciprocal",
                                    dn_sb[:, hd * QB:(hd + 1) * QB],
                                    ps_dn[:, hd * QB:(hd + 1) * QB],
                                    deps=[dtok], inc=True)
                        pib, pmb, pdb = pool.next()
                        bt = em.mm(pmb[hd * 64:hd * 64 + 64, :QB],
                                   ones1x64[:],
                                   dn_sb[:, hd * QB:(hd + 1) * QB],
                                   True, True, deps=[rc, pdb], inc=True)
                        cpb = em.scal("copy", bc_sb[hb, :],
                                      pmb[hd * 64:hd * 64 + 64, :QB],
                                      deps=[bt], inc=True)
                        pool.done(pib, cpb)
                        if hd == 1:
                            ps_dn_use = rc
                    nt = em.vec("tensor_mul",
                                ctx_all[:, g, qb * QB:(qb + 1) * QB],
                                ps_full[:], bc_sb[:],
                                deps=[cpb, ctok], inc=True)
                    ps_full_use = nt
                    g_last = nt
                qp_use = g_last
                kp_use = g_last
                vtm_use = g_last

            ctx_done = em.bar("vector")
            wo_b.load(wo_slice(0))
            for t4 in range(4):
                csl = slice(t4 * TT, (t4 + 1) * TT)
                y_done = None
                for oc in range(NKC):
                    wi = (wo_b.i - 1) % 2
                    pi, pm, pdep = pool.next()
                    last = None
                    for gc in range(NKC):
                        last = em.mm(pm[:], wo_b.buf(wi)[:, gc, :],
                                     ctx_all[:, gc, csl], gc == 0,
                                     gc == NKC - 1,
                                     deps=([wo_b.tok(wi), ctx_done, pdep]
                                           if gc == 0 else []),
                                     inc=(gc == NKC - 1))
                    wo_b.used(wi, last)
                    wo_b.load(wo_slice((oc + 1) % NKC))
                    y_done = em.vec("tensor_add", yout[:, oc, :], pm[:],
                                    y_own[:, oc, csl],
                                    deps=[last, yld, yo_dma], inc=True)
                    pool.done(pi, y_done)
                yo_dma = em.dma("sync", yo_h[:, :, csl], yout[:],
                                deps=[em.bar("vector")])

            em.emit("gpsimd", lambda e: e.nop(), deps=[em.bar("sync")])
            em.finalize(block)
    return nc


def half_swap_matrix():
    P = np.zeros((128, 128), dtype=np.float32)
    for h in range(2):
        b = h * 64
        for j in range(32):
            P[b + j, b + j + 32] = 1.0
            P[b + j + 32, b + j] = 1.0
    return P.astype(BF)


def xpos_tables(positions, inverse, scale8):
    DH = 64
    half = np.arange(0, DH, 2, dtype=np.float64)
    inv_freq = 1.0 / (10000.0 ** (half / DH))
    t = positions.astype(np.float64)
    freqs = t[:, None] * inv_freq[None, :]
    base = (half + 0.4 * DH) / (1.4 * DH)
    power = (t - S // 2) / (QB // 2)
    scale = base[None, :] ** power[:, None]
    if inverse:
        scale = 1.0 / scale
    cos = np.cos(freqs) * scale
    sin = np.sin(freqs) * scale
    if scale8:
        cos = cos * 8.0
        sin = sin * 8.0
    Cc = np.concatenate([cos.T, cos.T], axis=0)
    Ss = np.concatenate([-sin.T, sin.T], axis=0)
    return (np.tile(Cc, (2, 1)).astype(BF),
            np.tile(Ss, (2, 1)).astype(BF))


def p6_masks():
    i = np.arange(128)[:, None]
    j = np.arange(QB)[None, :]
    T1 = np.where(j <= i, 0.0, -1e30)
    T2 = np.where(j <= i + 128, 0.0, -1e30)
    T3 = np.where(j >= i, 0.0, -1e30)
    T4 = np.where(j >= i + 128, 0.0, -1e30)
    return np.stack([T1, T2, T3, T4]).astype(np.float32)


def head_indicator():
    hdi = np.zeros((128, 2), np.float32)
    hdi[0:64, 0] = 1.0
    hdi[64:128, 1] = 1.0
    hdi2 = np.zeros((2, 128), np.float32)
    hdi2[0, 0:64] = 1.0
    hdi2[1, 64:128] = 1.0
    return hdi, hdi2


def host_prep_p6(inputs, layer, y_full):
    blocks128 = [(i * 128, (i + 1) * 128) for i in range(NKC)]
    Wqkv = np.ascontiguousarray(np.asarray(inputs["L1_ffn_Wqkv"][layer],
                                           np.float32))
    wq_img = np.concatenate(
        [tile_w(np.ascontiguousarray(Wqkv[:, p * D:(p + 1) * D]), blocks128)
         for p in range(3)], axis=1)
    Wo = np.ascontiguousarray(np.asarray(inputs["L1_ffn_Wo"][layer],
                                         np.float32))
    wo_img = tile_w(Wo, blocks128)
    qs = np.asarray(inputs["L1_ffn_q_scale"][layer], np.float32)
    ks = np.asarray(inputs["L1_ffn_k_scale"][layer], np.float32)
    qs_rot = np.concatenate([qs[32:], qs[:32]])
    ks_rot = np.concatenate([ks[32:], ks[:32]])
    qs_img = np.stack([np.tile(qs, 2), np.tile(qs_rot, 2),
                       np.tile(ks, 2), np.tile(ks_rot, 2)], axis=1)
    qs_img = np.ascontiguousarray(qs_img.astype(np.float32))
    masks = p6_masks()
    mc = np.ascontiguousarray(masks.transpose(1, 0, 2).astype(BF))
    hsw = half_swap_matrix()
    hdi, hdi2 = head_indicator()

    in_maps = []
    for c in range(NCORES):
        b, h = divmod(c, 2)
        a = h * TPC
        qpos = a + np.arange(TPC)
        kpos = np.maximum(a - QB + np.arange(QB + TPC), 0)
        cSq, sSq = xpos_tables(qpos, False, True)
        cIk, sIk = xpos_tables(kpos, True, False)
        ysl = np.zeros((D, P6W), np.float32)
        lo = b * S + a - 258
        src_lo = max(lo, b * S)
        pad = src_lo - lo
        ysl[:, pad:] = y_full[:, src_lo:lo + P6W]
        if h == 0:
            mf = np.full((2, 128, QB), -1e30, np.float32)
        else:
            mf = masks[0:2]
        mf_img = np.ascontiguousarray(mf.transpose(1, 0, 2).astype(BF))
        in_maps.append({
            "y": to_fm(ysl), "wqkv": wq_img, "wo": wo_img,
            "cSq": np.ascontiguousarray(cSq), "sSq": np.ascontiguousarray(sSq),
            "cIk": np.ascontiguousarray(cIk), "sIk": np.ascontiguousarray(sIk),
            "qs": qs_img, "mconst": mc, "mfirst": mf_img,
            "hsw": hsw, "hdi": hdi, "hdi2": hdi2,
        })
    return in_maps



P7W = TPC + 1
HB = 22
HBW = [128] * 21 + [42]


def build_p7(final_add=False):
    nc = bass.Bass("TRN2", target_bir_lowering=False)
    y_h = nc.dram_tensor("y", [128, NKC, P7W], F32, kind="ExternalInput").ap()
    w1_h = nc.dram_tensor("w1", [128, sum(NKC * 2 * w for w in HBW)], F32,
                          kind="ExternalInput").ap()
    w2_h = nc.dram_tensor("w2", [128, HB * NKC * 128], F32,
                          kind="ExternalInput").ap()
    g_h = nc.dram_tensor("g", [128, NKC], F32, kind="ExternalInput").ap()
    bl_h = nc.dram_tensor("bl", [128, NKC], F32, kind="ExternalInput").ap()
    b1a_h = nc.dram_tensor("b1a", [128, HB], F32, kind="ExternalInput").ap()
    b1g_h = nc.dram_tensor("b1g", [128, HB], F32, kind="ExternalInput").ap()
    b2_h = nc.dram_tensor("b2", [128, NKC], F32, kind="ExternalInput").ap()
    if final_add:
        xe_h = nc.dram_tensor("xe", [128, NKC, TPC], F32,
                              kind="ExternalInput").ap()
    yo_h = nc.dram_tensor("yo", [128, NKC, TPC], F32,
                          kind="ExternalOutput").ap()

    with contextlib.ExitStack() as ctx:
        em = Em(nc)
        em.init_sems(ctx)
        sb = lambda name, shape, dt: ctx.enter_context(
            nc.sbuf_tensor(name, shape, dt))
        psa = lambda name, shape, dt=F32: ctx.enter_context(
            nc.psum_tensor(name, shape, dt))

        y_sb = sb("y_sb", [128, NKC, P7W], F32)
        xnb = sb("xnb", [128, NKC, TT], BF16)
        hbuf = sb("hbuf", [128, HB, TT], BF16)
        gel = sb("gel", [128, TT], F32)
        yout = sb("yout", [128, NKC, TT], F32)
        if final_add:
            xe_t = sb("xe_t", [128, NKC, TT], F32)
        w2_res = sb("w2_res", [128, HB, NKC, 128], BF16)
        sqbuf = [sb(f"sq{i}", [128, TT], F32) for i in range(2)]
        ones128 = sb("ones128", [128, 1], F32)
        ones1r = sb("ones1r", [1, 128], F32)
        eps_col = sb("eps_col", [1, 1], F32)
        rowbuf = sb("rowbuf", [1, 5 * TT], F32)
        tmp1 = sb("tmp1", [128, TT], F32)
        cols = {n: sb(f"col_{n}", [128, NKC], F32) for n in ("g", "bl", "b2")}
        colh = {n: sb(f"col_{n}", [128, HB], F32) for n in ("b1a", "b1g")}
        w1_b = WBuf(em, [sb(f"w1{i}", [128, NKC, 256], BF16)
                         for i in range(2)])

        ps_mu = psa("ps_mu", [1, TT])
        ps_var = psa("ps_var", [1, TT])
        ps_brstd = psa("ps_brstd", [128, TT])
        ps_bmur = psa("ps_bmur", [128, TT])
        pool = PsPool([psa(f"pp{i}", [128, TT]) for i in range(4)])

        w1off = []
        o = 0
        for w in HBW:
            w1off.append(o)
            o += NKC * 2 * w

        def w1_slice(blk):
            w = HBW[blk]
            return w1_h[:, w1off[blk]:w1off[blk] + NKC * 2 * w].rearrange(
                "p (c n) -> p c n", c=NKC)

        with nc.Block() as block:
            for n, hh in (("g", g_h), ("bl", bl_h), ("b2", b2_h)):
                em.dma("sync", cols[n][:], hh[:])
            for n, hh in (("b1a", b1a_h), ("b1g", b1g_h)):
                cld = em.dma("sync", colh[n][:], hh[:])
            yld = em.dma("sync", y_sb[:], y_h[:])
            w2ld = em.dma("gpsimd", w2_res[:], w2_h[:].rearrange(
                "p (b c n) -> p b c n", b=HB, c=NKC))
            em.emit("vector", lambda e: e.memset(ones128[:], 1.0), inc=1)
            em.emit("vector", lambda e: e.memset(ones1r[:], 1.0), inc=1)
            em.emit("vector", lambda e: e.memset(eps_col[:], 1e-5), inc=1)
            consts = em.bar("vector")
            em.emit("tensor", lambda e: e.nop(), deps=[cld, yld, w2ld, consts])
            em.emit("scalar", lambda e: e.nop(), deps=[cld, yld, consts])
            em.emit("vector", lambda e: e.nop(), deps=[cld, yld])

            state = {"tmp128": tmp1, "mu_free": consts, "var_free": consts,
                     "b_free": consts}
            xnb_free = None
            h_free = None
            yo_dma = None
            gel_use = None
            xel = None

            for tf in range(NTT):
                o2 = tf * TT
                xw = [y_sb[:, kc, (o2 + 1 if kc < 4 else o2):][:, :TT]
                      for kc in range(NKC)]
                state["xn_free"] = [xnb_free] if xnb_free else None
                xn_tok = emit_ln(
                    em, ps_mu, ps_var, ps_brstd, ps_bmur, ones128, ones1r,
                    rowbuf, sqbuf, eps_col, xw, TT,
                    lambda kc: xnb[:, kc, :], yld, state,
                    gain_bias=(cols["g"], cols["bl"]))
                w1_b.load(w1_slice(0),
                          dst_fn=lambda b: b[:, :, :2 * HBW[0]])
                for blk in range(HB):
                    wcur = (w1_b.i - 1) % 2
                    w = HBW[blk]
                    pia, pma, pda = pool.next()
                    pig, pmg, pdg = pool.next()
                    la = None
                    lg = None
                    for kc in range(NKC):
                        la = em.mm(pma[:w, :], w1_b.buf(wcur)[:, kc, 0:w],
                                   xnb[:, kc, :], kc == 0, kc == NKC - 1,
                                   deps=([w1_b.tok(wcur), xn_tok, pda, h_free]
                                         if kc == 0 else []),
                                   inc=(kc == NKC - 1))
                    for kc in range(NKC):
                        lg = em.mm(pmg[:w, :],
                                   w1_b.buf(wcur)[:, kc, w:2 * w],
                                   xnb[:, kc, :], kc == 0, kc == NKC - 1,
                                   deps=[pdg] if kc == 0 else [],
                                   inc=(kc == NKC - 1))
                    w1_b.used(wcur, lg)
                    gt = em.act(gel[:w, :], pmg[:w, :], AF.Gelu,
                                bias=colh["b1g"][:w, blk:blk + 1],
                                deps=[lg, gel_use], inc=True)
                    ht = em.vec("scalar_tensor_tensor",
                                hbuf[:w, blk, :], pma[:w, :],
                                colh["b1a"][:w, blk:blk + 1], gel[:w, :],
                                ALU.add, ALU.mult,
                                deps=[la, gt], inc=True)
                    gel_use = ht
                    pool.done(pia, ht)
                    pool.done(pig, ht)
                    if blk < HB - 1:
                        w1_b.load(w1_slice(blk + 1),
                                  dst_fn=lambda b, w2=HBW[blk + 1]:
                                  b[:, :, :2 * w2])
                h_tok = em.bar("vector")
                csl = slice(o2, o2 + TT)
                if final_add:
                    xel = em.dma("sync", xe_t[:], xe_h[:, :, csl],
                                 deps=[yo_dma])
                for oc in range(NKC):
                    pi, pm, pdep = pool.next()
                    last = None
                    for blk in range(HB):
                        w = HBW[blk]
                        last = em.mm(pm[:], w2_res[0:w, blk, oc, :],
                                     hbuf[0:w, blk, :], blk == 0,
                                     blk == HB - 1,
                                     deps=([h_tok, pdep] if blk == 0 else []),
                                     inc=(blk == HB - 1))
                    yt = em.vec("scalar_tensor_tensor", yout[:, oc, :],
                                pm[:], cols["b2"][:, oc:oc + 1],
                                y_sb[:, oc, 1 + o2:1 + o2 + TT],
                                ALU.add, ALU.add,
                                deps=[last, yo_dma], inc=True)
                    if final_add:
                        yt = em.vec("tensor_add", yout[:, oc, :],
                                    yout[:, oc, :], xe_t[:, oc, :],
                                    deps=[xel], inc=True)
                    pool.done(pi, yt)
                yo_dma = em.dma("sync", yo_h[:, :, csl], yout[:],
                                deps=[em.bar("vector")])
                xnb_free = em.bar("tensor")
                h_free = em.bar("tensor")

            em.emit("gpsimd", lambda e: e.nop(), deps=[em.bar("sync")])
            em.finalize(block)
    return nc


def host_prep_p7(inputs, layer, yp_full, xe_full=None):
    W1 = np.asarray(inputs["L1_ffn_W1"][layer], np.float32)
    W2 = np.asarray(inputs["L1_ffn_W2"][layer], np.float32)
    b1 = np.asarray(inputs["L1_ffn_b1"][layer], np.float32)
    DI = 2730
    w1_parts = []
    b1a = np.zeros((128, HB), np.float32)
    b1g = np.zeros((128, HB), np.float32)
    r0 = 0
    for blk, w in enumerate(HBW):
        ta = tile_w(np.ascontiguousarray(W1[:, r0:r0 + w]),
                    [(0, w)]).reshape(128, NKC, w)
        tg = tile_w(np.ascontiguousarray(W1[:, DI + r0:DI + r0 + w]),
                    [(0, w)]).reshape(128, NKC, w)
        w1_parts.append(np.concatenate([ta, tg], axis=2).reshape(128, -1))
        b1a[:w, blk] = b1[r0:r0 + w]
        b1g[:w, blk] = b1[DI + r0:DI + r0 + w]
        r0 += w
    w1_img = np.ascontiguousarray(np.concatenate(w1_parts, axis=1))
    w2_parts = []
    r0 = 0
    for blk, w in enumerate(HBW):
        pad = np.zeros((128, D), np.float32)
        pad[:w] = W2[r0:r0 + w]
        w2_parts.append(pad.reshape(128, NKC, 128))
        r0 += w
    w2_img = np.ascontiguousarray(
        np.concatenate(w2_parts, axis=1).reshape(128, -1))
    g_img = col_img(np.asarray(inputs["L1_ffn_ln_g"][layer]))
    bl_img = col_img(np.asarray(inputs["L1_ffn_ln_b"][layer]))
    b2_img = col_img(np.asarray(inputs["L1_ffn_b2"][layer]))

    in_maps = []
    for c in range(NCORES):
        b, h = divmod(c, 2)
        a = h * TPC
        ysl = np.zeros((D, P7W), np.float32)
        lo = b * S + a - 1
        src_lo = max(lo, b * S)
        pad = src_lo - lo
        ysl[:, pad:] = yp_full[:, src_lo:lo + P7W]
        m = {"y": to_fm(ysl), "w1": w1_img, "w2": w2_img,
             "g": g_img, "bl": bl_img, "b1a": b1a, "b1g": b1g, "b2": b2_img}
        if xe_full is not None:
            m["xe"] = to_fm(xe_full[:, c * TPC:(c + 1) * TPC])
        in_maps.append(m)
    return in_maps
